# version 17
# baseline (speedup 1.0000x reference)
import contextlib
import numpy as np
import concourse.bass as bass
import concourse.mybir as mybir
from concourse.bass_utils import run_bass_kernel_spmd

F32 = mybir.dt.float32
BF16 = mybir.dt.bfloat16
AF = mybir.ActivationFunctionType
ALU = mybir.AluOpType
AX = mybir.AxisListType

D = 2048
NCH = 16
T = 2048
TT = 512
NT = T // TT
NU = TT // 128
H = 16
G = 4
HD = 128
E8 = 8
EI = 128
LN_EPS = 1e-5
AT_EPS = 1e-6


def build_nc():
    nc = bass.Bass("TRN2", target_bir_lowering=False)

    xin = nc.dram_tensor("XT", [D, T], F32, kind="ExternalInput").ap()
    wqt = nc.dram_tensor("WQT", [D, D], F32, kind="ExternalInput").ap()
    wkvt = nc.dram_tensor("WKVT", [D, 1024], F32, kind="ExternalInput").ap()
    wot = nc.dram_tensor("WOT", [D, D], F32, kind="ExternalInput").ap()
    rwt = nc.dram_tensor("RWT", [D, E8], F32, kind="ExternalInput").ap()
    dwt = nc.dram_tensor("DWT", [D, EI], F32, kind="ExternalInput").ap()
    uwt = nc.dram_tensor("UWT", [EI, D], F32, kind="ExternalInput").ap()
    bqrow = nc.dram_tensor("BQROW", [1, D], F32, kind="ExternalInput").ap()
    bkvrow = nc.dram_tensor("BKVROW", [1, 1024], F32, kind="ExternalInput").ap()
    out_h = nc.dram_tensor("OUT", [D, T], F32, kind="ExternalOutput").ap()
    wqbf_h = nc.dram_tensor("WQBF", [8, 128, 4096], BF16, kind="Internal").ap()
    xlnbf_h = nc.dram_tensor("XLNBF", [D, T], BF16, kind="Internal").ap()
    ccin = nc.dram_tensor("CCIN", [128, G * 129], F32, kind="Internal")
    ccout = nc.dram_tensor("CCOUT", [128, G * 129], F32, kind="Internal")
    ccin2 = nc.dram_tensor("CCIN2", [128, G * 129], F32, kind="Internal")
    ccout2 = nc.dram_tensor("CCOUT2", [128, G * 129], F32, kind="Internal")

    es = contextlib.ExitStack()
    sb = lambda name, shape, dt: es.enter_context(nc.sbuf_tensor(name, shape, dt))
    psf = lambda name: es.enter_context(nc.psum_tensor(name, [128, 512], F32))

    with es:
        XT = sb("XTs", [128, NCH * TT], F32)
        XLN = sb("XLN", [128, NCH * TT], BF16)
        ATTN = sb("ATTN", [128, H * TT], BF16)
        WO = sb("WO", [128, NCH * D], BF16)
        FFNW = sb("FFNW", [128, NCH * (E8 + EI) + D], BF16)
        KVBF = sb("KVBF", [128, G * HD], BF16)
        KSUM = sb("KSUM", [128, G], BF16)
        KVROWS = sb("KVROWS", [128, G * 129], F32)
        KVR1 = sb("KVR1", [128, G * 129], F32)
        RSTD = sb("RSTD", [1, TT], BF16)
        NMR = sb("NMR", [1, TT], BF16)
        INVNR = sb("INVNR", [1, 2 * TT], BF16)
        R32 = sb("R32", [1, 4 * TT], F32)
        ONESC = sb("ONESC", [128, 1], BF16)
        ONESR = sb("ONESR", [1, 128], BF16)
        ONES5 = sb("ONES5", [1, TT], BF16)
        REP = sb("REP", [8, 128], BF16)
        ID32 = sb("ID32", [128, 128], F32)
        BQR = sb("BQR", [1, D], BF16)
        BKVR = sb("BKVR", [1, 1024], BF16)
        WSRW = sb("WSRW", [1, E8], BF16)
        WSDW = sb("WSDW", [1, EI], BF16)
        NMRB = sb("NMRB", [128, TT], F32)
        LG8 = sb("LG8", [8, TT], F32)
        GS = sb("GS", [128, 128], F32)
        STG0 = sb("STG0", [128, TT], F32)
        STG1 = sb("STG1", [128, TT], F32)
        SCRB = sb("SCRB", [128, 18432], BF16)

        WKV = SCRB[:, 0:16384]
        SKB = [SCRB[:, 16384:16896], SCRB[:, 16896:17408]]
        WQH = [SCRB[:, s * 4096:(s + 1) * 4096] for s in range(2)]
        NSQ = 7
        SQB = [SCRB[:, 8192 + i * 512:8704 + i * 512] for i in range(NSQ)]
        INVB = [SCRB[:, 11776 + i * 512:12288 + i * 512] for i in range(2)]
        GF8 = SCRB[0:8, 12800:13312]
        ABF = SCRB[:, 13312:13824]
        AB2 = SCRB[:, 13824:14336]
        RSB = SCRB[:, 17408:17920]
        EXPB = SCRB[:, 17920:18432]
        EFF = STG0[:, 0:TT]
        RFF = STG1[:, 0:TT]

        XBFa = [ATTN[:, 0:512], ATTN[:, 512:1024]]
        XSQa = [ATTN[:, 1024:1536], ATTN[:, 1536:2048]]
        XBFb = [SCRB[:, 14336:14848], SCRB[:, 14848:15360]]
        XSQb = [SCRB[:, 15360:15872], SCRB[:, 15872:16384]]
        EXPa = [ATTN[:, 2048:2560], ATTN[:, 2560:3072]]
        RSBa = [ATTN[:, 3072:3584], ATTN[:, 3584:4096]]
        VOB = [ATTN[:, 4096:4616], ATTN[:, 4616:5136]]

        ROW0 = R32[:, 0:TT]
        ROW1 = R32[:, TT:2 * TT]
        ROW2 = R32[:, 2 * TT:3 * TT]
        ROW3 = R32[:, 3 * TT:4 * TT]

        FFN_RW = FFNW[:, 0:NCH * E8]
        FFN_DW = FFNW[:, NCH * E8:NCH * (E8 + EI)]
        FFN_UW = FFNW[:, NCH * (E8 + EI):NCH * (E8 + EI) + D]

        LGT = [GS[:, c * 8:(c + 1) * 8] for c in range(4)]
        WTS = [GS[:, 32 + c * 8:32 + (c + 1) * 8] for c in range(4)]
        GT = [GS[:, 64 + c * 8:64 + (c + 1) * 8] for c in range(4)]
        GM1 = [GS[:, 96 + c:97 + c] for c in range(4)]
        GM2 = [GS[:, 100 + c:101 + c] for c in range(4)]
        GSM = [GS[:, 104 + c:105 + c] for c in range(4)]

        P = [psf(f"P{i}") for i in range(8)]
        AKV = P[0:4]

        ses = contextlib.ExitStack()

        class Sem:
            def __init__(self, name):
                self.h = ses.enter_context(nc.semaphore(name))
                self.n = 0

        dXT = Sem("dXT")
        dXL = Sem("dXL")
        dWB = Sem("dWB")
        dOUT = Sem("dOUT")
        dCC = Sem("dCC")
        dCC2 = Sem("dCC2")
        dQs = [Sem("dQ0"), Sem("dQ1")]
        dGW = Sem("dGW")
        dGF = Sem("dGF")
        dGO = Sem("dGO")
        dGQ = [Sem("dGQ0"), Sem("dGQ1")]
        dGB = [Sem("dGB0"), Sem("dGB1")]
        sT = Sem("sT")
        sV = Sem("sV")
        sS = Sem("sS")
        sG = Sem("sG")
        sC = Sem("sC")

        with ses:
            plan = {k: [] for k in ("sync", "tensor", "vector", "scalar", "gpsimd")}

            def op(eng, fn, inc=None, inck=1):
                if inc is None:
                    plan[eng].append(lambda e, fn=fn: fn(e))
                    return None
                tgt = inc.n + inck
                inc.n = tgt
                plan[eng].append(lambda e, fn=fn, s=inc, k=inck: fn(e).then_inc(s.h, k))
                return tgt

            def wait(eng, sem, val=None):
                v = sem.n if val is None else val
                assert v <= sem.n, f"wait on future value {v} > {sem.n}"
                if v > 0:
                    plan[eng].append(lambda e, s=sem, v=v: e.wait_ge(s.h, v))
                return v

            def dma(q_eng, sem, outap, inap):
                return op(q_eng, lambda e, o=outap, i=inap: e.dma_start(out=o, in_=i),
                          inc=sem, inck=16)

            def mm(pq, lhsT, rhs, start, stop, inc=None):
                return op("tensor",
                          lambda e, o=pq, l=lhsT, r=rhs, a=start, b=stop:
                          e.matmul(o, l, r, start=a, stop=b), inc=inc)

            op("vector", lambda e: e.memset(ONESC[:], 1.0))
            op("vector", lambda e: e.memset(ONESR[:], 1.0))
            op("vector", lambda e: e.memset(ONES5[:], 1.0))
            op("vector", lambda e: e.memset(REP[:], 1.0))
            op("vector", lambda e: e.memset(ID32[:], 1.0))
            for i in range(2):
                op("vector", lambda e, i=i: e.memset(VOB[i], 0.0), inc=sV)
            wait("vector", sV)
            for i in range(2):
                for g in range(G):
                    op("vector", lambda e, i=i, g=g: e.memset(
                        VOB[i][:, g * 130 + 128:g * 130 + 129], 1.0),
                       inc=(sV if (i, g) == (1, G - 1) else None))
            v_const = sV.n
            wait("gpsimd", sV, v_const)
            op("gpsimd", lambda e: e.affine_select(
                REP[:].rearrange("p (j i) -> p j i", i=16),
                REP[:].rearrange("p (j i) -> p j i", i=16),
                pattern=[[-1, 8], [0, 16]], compare_op=ALU.is_equal, fill=0.0,
                base=0, channel_multiplier=1))
            op("gpsimd", lambda e: e.affine_select(
                ID32[:], ID32[:], pattern=[[-1, 128]], compare_op=ALU.is_equal,
                fill=0.0, base=0, channel_multiplier=1), inc=sG)
            g_const = sG.n

            wait("gpsimd", sG, g_const)
            for k in range(4):
                op("gpsimd", lambda e, k=k: e.dma_start(
                    out=BQR[:, k * TT:(k + 1) * TT], in_=bqrow[:, k * TT:(k + 1) * TT]),
                   inc=dGF, inck=16)
            for k in range(2):
                op("gpsimd", lambda e, k=k: e.dma_start(
                    out=BKVR[:, k * TT:(k + 1) * TT], in_=bkvrow[:, k * TT:(k + 1) * TT]),
                   inc=dGF, inck=16)
            for c in range(NCH):
                op("gpsimd", lambda e, c=c: e.dma_start(
                    out=WKV[:, c * 1024:(c + 1) * 1024], in_=wkvt[c * 128:(c + 1) * 128, :]),
                   inc=dGW, inck=16)
            d_wkv = dGW.n
            for c in range(NCH):
                op("gpsimd", lambda e, c=c: e.dma_start(
                    out=FFN_RW[:, c * E8:(c + 1) * E8], in_=rwt[c * 128:(c + 1) * 128, :]),
                   inc=dGF, inck=16)
                op("gpsimd", lambda e, c=c: e.dma_start(
                    out=FFN_DW[:, c * EI:(c + 1) * EI], in_=dwt[c * 128:(c + 1) * 128, :]),
                   inc=dGF, inck=16)
            op("gpsimd", lambda e: e.dma_start(out=FFN_UW, in_=uwt[0:128, :]), inc=dGF, inck=16)
            d_ffnw = dGF.n

            wait("tensor", dGF, d_ffnw)
            wait("tensor", sV, v_const)
            for c in range(NCH):
                mm(P[6][0:1, 0:E8], ONESC[:], FFN_RW[:, c * E8:(c + 1) * E8],
                   c == 0, c == NCH - 1)
            t_ws = 0
            for c in range(NCH):
                t_ws = mm(P[7][0:1, 0:EI], ONESC[:], FFN_DW[:, c * EI:(c + 1) * EI],
                          c == 0, c == NCH - 1, inc=(sT if c == NCH - 1 else None))
            wait("vector", sT, t_ws)
            op("vector", lambda e: e.tensor_copy(WSRW[:], P[6][0:1, 0:E8]))
            v_ws = op("vector", lambda e: e.tensor_copy(WSDW[:], P[7][0:1, 0:EI]), inc=sV)

            wo_c = [0]

            def wo_flow():
                c = wo_c[0]
                wo_c[0] += 1
                op("gpsimd", lambda e, c=c: e.dma_start(
                    out=WO[:, c * D:(c + 1) * D], in_=wot[c * 128:(c + 1) * 128, :]),
                   inc=dGO, inck=16)

            stats_guard = {"sq": {0: 0, 1: 0}, "mm1": {0: 0, 1: 0}, "mm2": {0: 0, 1: 0},
                           "cast": {0: 0, 1: 0}}

            def ln_stats_chunk(c, pb1, pb2, pre_tensor_waits=None, cast_wait=None,
                               XBFd=XBFa, XSQd=XSQa):
                i = c % 2
                wait("gpsimd", sS, stats_guard["sq"][i])
                wait("gpsimd", sT, stats_guard["mm1"][i])
                if cast_wait is not None:
                    cast_wait()
                v_cast = op("gpsimd", lambda e, c=c, i=i: e.tensor_copy(
                    XBFd[i], XT[:, c * TT:(c + 1) * TT]), inc=sG)
                stats_guard["cast"][i] = v_cast
                wait("scalar", sG, v_cast)
                wait("scalar", sT, stats_guard["mm2"][i])
                last_sq = op("scalar", lambda e, i=i: e.activation(XSQd[i], XBFd[i], AF.Square),
                             inc=sS)
                stats_guard["sq"][i] = last_sq
                wait("tensor", sG, v_cast)
                wait("tensor", sS, last_sq)
                if pre_tensor_waits is not None and c == 0:
                    pre_tensor_waits()
                stats_guard["mm1"][i] = mm(pb1[0:1, 0:TT], ONESC[:], XBFd[i],
                                           c == 0, c == NCH - 1, inc=sT)
                last_mm2 = mm(pb2[0:1, 0:TT], ONESC[:], XSQd[i], c == 0, c == NCH - 1, inc=sT)
                stats_guard["mm2"][i] = last_mm2
                return last_mm2, last_sq

            def ln_stats(pb1, pb2, pre_tensor_waits=None, cast_wait=None):
                last_mm2 = last_sq = 0
                for c in range(NCH):
                    last_mm2, last_sq = ln_stats_chunk(c, pb1, pb2, pre_tensor_waits, cast_wait)
                return last_mm2, last_sq

            def ln_finalize(pb1, pb2, t_stats):
                wait("vector", sT, t_stats)
                op("vector", lambda e: e.tensor_scalar_mul(ROW0, pb1[0:1, 0:TT], 1.0 / D))
                op("vector", lambda e: e.tensor_scalar_mul(ROW1, pb2[0:1, 0:TT], 1.0 / D), inc=sV)
                wait("vector", sV)
                op("vector", lambda e: e.tensor_tensor(ROW2, ROW0, ROW0, ALU.mult), inc=sV)
                wait("vector", sV)
                op("vector", lambda e: e.tensor_tensor(ROW1, ROW1, ROW2, ALU.subtract), inc=sV)
                wait("vector", sV)
                v_var = op("vector", lambda e: e.tensor_scalar_add(ROW1, ROW1, LN_EPS), inc=sV)
                wait("scalar", sV, v_var)
                op("scalar", lambda e: e.activation(ROW2, ROW1, AF.Ln), inc=sS)
                wait("scalar", sS)
                s_rstd = op("scalar", lambda e: e.activation(RSTD[:], ROW2, AF.Exp, scale=-0.5),
                            inc=sS)
                wait("vector", sS, s_rstd)
                op("vector", lambda e: e.tensor_tensor(NMR[:], ROW0, RSTD[:], ALU.mult), inc=sV)
                wait("vector", sV)
                v_fin = op("vector", lambda e: e.tensor_scalar_mul(NMR[:], NMR[:], -1.0), inc=sV)
                return v_fin

            evac_s = {0: 0, 1: 0}
            evac_v = {0: 0, 1: 0}
            kvstate_done = {0: 0, 1: 0}
            kvstate_t = 0
            g_nmrb_read = 0
            wb_prev = 0

            d_x = dma("sync", dXT,
                      XT[:].rearrange("p (c t) -> p c t", t=TT),
                      xin.rearrange("(c p) t -> p c t", p=128)[:, :, 0:TT])
            wait("gpsimd", dXT, d_x)
            t_stats, _ = ln_stats(P[4], P[5])
            v_fin = ln_finalize(P[4], P[5], t_stats)

            cc_split = {}
            for t in range(NT):
                if t == NT - 1:
                    wait("scalar", sT, kvstate_t)
                    for g in range(G):
                        op("scalar", lambda e, g=g: e.copy(
                            KVROWS[:, g * 129:(g + 1) * 129], AKV[g][:, 0:129]),
                           inc=(sS if g == G - 1 else None))
                    cc_split["evac1"] = sS.n
                    wait("scalar", sS, cc_split["evac1"])
                    cc_split["ccin1"] = dma("scalar", dCC, ccin.ap(), KVROWS[:])
                wait("tensor", sV, v_fin)
                mm(P[4][:, 0:TT], ONESR[:], RSTD[:], True, True)
                t_bc = mm(P[5][:, 0:TT], ONESR[:], NMR[:], True, True, inc=sT)
                wait("scalar", sT, t_bc)
                if t > 0:
                    wait("scalar", sG, g_nmrb_read)
                s_nmrb = op("scalar", lambda e: e.copy(NMRB[:], P[5][:, 0:TT]), inc=sS)
                wait("vector", sT, t_bc)
                if t > 0:
                    wait("vector", dWB, wb_prev)
                mult_v = {}
                for c in range(NCH):
                    mult_v[c] = op("vector", lambda e, c=c: e.tensor_tensor(
                        XLN[:, c * TT:(c + 1) * TT], XT[:, c * TT:(c + 1) * TT],
                        P[4][:, 0:TT], ALU.mult), inc=sV)
                v_scale = mult_v[NCH - 1]
                add_sem = {}
                wait("gpsimd", sS, s_nmrb)
                for c in range(8):
                    wait("gpsimd", sV, mult_v[c])
                    add_sem[c] = (sG, op("gpsimd", lambda e, c=c: e.tensor_tensor(
                        XLN[:, c * TT:(c + 1) * TT], XLN[:, c * TT:(c + 1) * TT],
                        NMRB[:], ALU.add), inc=sG))
                g_ln1 = sG.n
                g_nmrb_read = g_ln1
                wait("vector", sS, s_nmrb)
                for c in range(8, NCH):
                    wait("vector", sV, mult_v[c])
                    add_sem[c] = (sV, op("vector", lambda e, c=c: e.tensor_tensor(
                        XLN[:, c * TT:(c + 1) * TT], XLN[:, c * TT:(c + 1) * TT],
                        NMRB[:], ALU.add), inc=sV))
                v_ln1 = sV.n
                wait("sync", sG, g_ln1)
                wait("sync", sV, v_ln1)
                wb_prev = dma("sync", dWB,
                    xlnbf_h.rearrange("(c p) t -> p c t", p=128)[:, :, t * TT:(t + 1) * TT],
                    XLN[:].rearrange("p (c t) -> p c t", t=TT))
                if t + 1 < NT:
                    wait("sync", sV, v_scale)
                    d_x = dma("sync", dXT,
                              XT[:].rearrange("p (c t) -> p c t", t=TT),
                              xin.rearrange("(c p) t -> p c t", p=128)[:, :, (t + 1) * TT:(t + 2) * TT])

                if t == 0:
                    wait("tensor", dGW, d_wkv)
                    wait("tensor", dGF, d_ffnw)
                tkv = {}
                for u in range(NU):
                    kb = P[6] if u % 2 == 0 else P[4]
                    vb = P[7] if u % 2 == 0 else P[5]
                    i = u % 2
                    wait("tensor", sS, evac_s[i])
                    wait("tensor", sV, evac_v[i])
                    tok = u * 128
                    for c in range(NCH):
                        if u == 0:
                            wait("tensor", add_sem[c][0], add_sem[c][1])
                        lhs = XLN[:, c * TT + tok:c * TT + tok + 128]
                        mm(kb[:, 0:512], lhs, WKV[:, c * 1024:c * 1024 + 512], c == 0, False)
                        mm(vb[:, 0:512], lhs, WKV[:, c * 1024 + 512:(c + 1) * 1024], c == 0, False)
                    mm(kb[:, 0:512], ONESR[:], BKVR[:, 0:512], False, True)
                    tkv[u] = mm(vb[:, 0:512], ONESR[:], BKVR[:, 512:1024], False, True, inc=sT)
                    if u > 0:
                        pi = (u - 1) % 2
                        wait("tensor", sV, evac_v[pi])
                        first = (u == 1 and t in (0, NT - 1))
                        if t == NT - 1 and u == 1:
                            wait("tensor", sS, cc_split["evac1"])
                        for g in range(G):
                            kvstate_t = mm(AKV[g][:, 0:129], SKB[pi][:, g * 128:(g + 1) * 128],
                                           VOB[pi][:, g * 130:g * 130 + 129], first, False,
                                           inc=(sT if g == G - 1 else None))
                        kvstate_done[pi] = kvstate_t
                    wait("vector", sT, tkv[u])
                    wait("vector", sT, kvstate_done[i])
                    op("vector", lambda e, vb=vb, i=i: e.tensor_copy(
                        VOB[i].rearrange("p (g c) -> p g c", c=130)[:, :, 0:128],
                        vb[:, 0:512].rearrange("p (g c) -> p g c", c=128)))
                    wait("vector", sV, evac_v[i])
                    v_rsa = op("vector", lambda e, kb=kb, i=i: e.tensor_scalar_max(
                        RSBa[i], kb[:, 0:512], 0.0), inc=sV)
                    wait("scalar", sV, v_rsa)
                    s_ev = op("scalar", lambda e, kb=kb, i=i: e.activation(
                        EXPa[i], kb[:, 0:512], AF.Exp), inc=sS)
                    evac_s[i] = s_ev
                    wait("vector", sS, s_ev)
                    ev = op("vector", lambda e, i=i: e.scalar_tensor_tensor(
                        SKB[i], EXPa[i], 1.0, RSBa[i], ALU.min, ALU.add), inc=sV)
                    evac_v[i] = ev
                pi = (NU - 1) % 2
                wait("tensor", sV, evac_v[pi])
                last = (t >= NT - 2)
                for g in range(G):
                    kvstate_t = mm(AKV[g][:, 0:129], SKB[pi][:, g * 128:(g + 1) * 128],
                                   VOB[pi][:, g * 130:g * 130 + 129], False, last,
                                   inc=(sT if g == G - 1 else None))
                kvstate_done[pi] = kvstate_t

                if t + 1 < NT:
                    ev1, es1 = evac_v[1], evac_s[1]
                    wait("gpsimd", dXT, d_x)
                    t_stats, _ = ln_stats(
                        P[4], P[5],
                        pre_tensor_waits=(lambda ev1=ev1, es1=es1: (
                            wait("tensor", sV, ev1), wait("tensor", sS, es1))))
                    wait("vector", sT, t_bc)
                    v_fin = ln_finalize(P[4], P[5], t_stats)

                for _ in range(4):
                    wo_flow()
                if t == NT - 1:
                    wait("gpsimd", dCC, cc_split["ccin1"])
                    cc1 = op("gpsimd", lambda e: e.collective_compute(
                        "AllReduce", ALU.add, replica_groups=[[0, 1, 2, 3], [4, 5, 6, 7]],
                        ins=[ccin.ap()], outs=[ccout.ap()]), inc=sC)
                    wait("gpsimd", sC, cc1)
                    cc_split["ccout1"] = op("gpsimd", lambda e: e.dma_start(
                        out=KVR1[:], in_=ccout.ap()), inc=dCC2, inck=16)

            wait("vector", sT, kvstate_t)
            wait("vector", dCC, cc_split["ccin1"])
            for g in range(G):
                op("vector", lambda e, g=g: e.tensor_copy(
                    KVROWS[:, g * 129:(g + 1) * 129], AKV[g][:, 0:129]),
                   inc=(sV if g == G - 1 else None))
            v_kvrows = sV.n
            wait("scalar", sV, v_kvrows)
            d_ccin = dma("scalar", dCC, ccin2.ap(), KVROWS[:])

            wait("sync", sV, v_scale)
            d_x0 = dma("sync", dXT,
                       XT[:].rearrange("p (c t) -> p c t", t=TT),
                       xin.rearrange("(c p) t -> p c t", p=128)[:, :, 0:TT])
            wait("sync", sT, tkv[NU - 1])
            wait("sync", dWB, wb_prev)
            d_xln0 = dma("sync", dXL,
                         XLN[:].rearrange("p (c t) -> p c t", t=TT),
                         xlnbf_h.rearrange("(c p) t -> p c t", p=128)[:, :, 0:TT])
            wqh_qmm = [0, 0]
            wqh_dma = [0, 0]

            def wqh_load(grp):
                slot = grp % 2
                if wqh_qmm[slot]:
                    wait("gpsimd", sT, wqh_qmm[slot])
                wqh_dma[slot] = op("gpsimd", lambda e, slot=slot, grp=grp: e.dma_start(
                    out=WQH[slot].rearrange("p (c o) -> p c o", o=256),
                    in_=wqt.rearrange("(c p) o -> p c o", p=128)[:, :, grp * 256:(grp + 1) * 256]),
                    inc=dQs[slot], inck=16)

            wait("gpsimd", sT, tkv[NU - 1])
            wqh_load(0)
            wqh_load(1)

            fT, fV, fS, fG = sT.n, sV.n, sS.n, sG.n

            cc_state = {}

            def emit_cc():
                wait("gpsimd", dCC, d_ccin)
                cc = op("gpsimd", lambda e: e.collective_compute(
                    "AllReduce", ALU.add, replica_groups=[[0, 1, 2, 3], [4, 5, 6, 7]],
                    ins=[ccin2.ap()], outs=[ccout2.ap()]), inc=sC)
                wait("gpsimd", sC, cc)
                d_ccout = op("gpsimd", lambda e: e.dma_start(out=KVROWS[:], in_=ccout2.ap()),
                             inc=dCC2, inck=16)
                wait("gpsimd", dCC2, d_ccout)
                g_sum = op("gpsimd", lambda e: e.tensor_tensor(
                    KVROWS[:], KVROWS[:], KVR1[:], ALU.add), inc=sG)
                wait("gpsimd", sG, g_sum)
                for g in range(G):
                    op("gpsimd", lambda e, g=g: e.tensor_copy(
                        KVBF[:, g * HD:(g + 1) * HD], KVROWS[:, g * 129:g * 129 + 128]))
                cc_state["v_kv"] = op("gpsimd", lambda e: e.tensor_copy(
                    KSUM[:], KVROWS[:].rearrange("p (g c) -> p g c", c=129)[:, :, 128]), inc=sG)

            emit_cc()

            for eng in ("tensor", "vector", "scalar", "sync", "gpsimd"):
                wait(eng, sT, fT)
                wait(eng, sV, fV)
                wait(eng, sS, fS)
                if eng != "gpsimd":
                    wait(eng, sG, fG)
            wait("tensor", dGO)
            wait("sync", dWB)

            xt_free = (None, 0)
            xln_rd = (None, 0)
            out_dma = 0
            pw_read = {0: 0, 1: 0}
            pq_read = {0: (0, 0), 1: (0, 0)}
            prev_stats_t = 0
            prev_stats_s = 0
            p2_read_s = 0
            p3_read_v = 0
            p6row_read_v = 0
            invb_read = {0: 0, 1: 0}
            sqb_qkv_t = {i: 0 for i in range(NSQ)}
            expb_read_v = 0
            attn_done_t = 0

            wait("tensor", sV, v_kvrows)
            vsq = {}
            d_x, d_xln = d_x0, d_xln0
            for t in range(NT):

                def do_q(h):
                    nonlocal expb_read_v
                    slot = (h // 2) % 2
                    j = h % 2
                    if j == 0 and not (t == 0 and h == 0):
                        wqh_load((h // 2 + 1) % 8)
                    pq = P[h % 2]
                    rS, rV = pq_read[h % 2]
                    wait("tensor", sS, rS)
                    wait("tensor", sV, rV)
                    wait("tensor", dQs[slot], wqh_dma[slot])
                    if h == 0:
                        wait("tensor", dXL, d_xln)
                    for c in range(NCH):
                        lhs = WQH[slot][:, c * 256 + j * 128:c * 256 + (j + 1) * 128]
                        mm(pq[:, 0:TT], lhs, XLN[:, c * TT:(c + 1) * TT], c == 0, False)
                    t_q = mm(pq[:, 0:TT], BQR[:, h * 128:(h + 1) * 128], ONES5[:], False, True,
                             inc=sT)
                    wqh_qmm[slot] = t_q
                    wait("vector", sT, t_q)
                    wait("vector", sT, sqb_qkv_t[h % NSQ])
                    wait("vector", sV, expb_read_v)
                    v_rsb = op("vector", lambda e, pq=pq: e.tensor_scalar_max(RSB, pq[:, 0:TT], 0.0),
                               inc=sV)
                    wait("scalar", sV, v_rsb)
                    s_q = op("scalar", lambda e, pq=pq: e.activation(EXPB, pq[:, 0:TT], AF.Exp),
                             inc=sS)
                    wait("vector", sS, s_q)
                    v_sq = op("vector", lambda e, h=h: e.scalar_tensor_tensor(
                        SQB[h % NSQ], EXPB, 1.0, RSB, ALU.min, ALU.add), inc=sV)
                    expb_read_v = v_sq
                    pq_read[h % 2] = (s_q, v_sq)
                    return v_sq

                def do_norm(h, v_sq):
                    nonlocal p6row_read_v
                    g = h // 4
                    wait("tensor", sV, v_sq)
                    if t == 0 and h == 0:
                        wait("tensor", sG, cc_state["v_kv"])
                    wait("tensor", sV, p6row_read_v)
                    t_n = mm(P[6][0:1, 0:TT], KSUM[:, g:g + 1], SQB[h % NSQ], True, True, inc=sT)
                    nrow = ROW0 if h % 2 == 0 else ROW1
                    lrow = ROW2 if h % 2 == 0 else ROW3
                    ihalf = INVNR[:, (h % 2) * TT:(h % 2 + 1) * TT]
                    wait("vector", sT, t_n)
                    v_n = op("vector", lambda e, nrow=nrow: e.tensor_scalar_add(
                        nrow, P[6][0:1, 0:TT], AT_EPS), inc=sV)
                    p6row_read_v = v_n
                    wait("scalar", sV, v_n)
                    op("scalar", lambda e, nrow=nrow, lrow=lrow: e.activation(lrow, nrow, AF.Ln),
                       inc=sS)
                    wait("scalar", sS)
                    s_i = op("scalar", lambda e, lrow=lrow, ih=ihalf: e.activation(
                        ih, lrow, AF.Exp, scale=-1.0), inc=sS)
                    return s_i

                def do_qkv(h, s_i):
                    nonlocal p2_read_s, p3_read_v
                    g = h // 4
                    ihalf = INVNR[:, (h % 2) * TT:(h % 2 + 1) * TT]
                    wait("tensor", sS, s_i)
                    wait("tensor", sS, p2_read_s)
                    t_ib = mm(P[2][:, 0:TT], ONESR[:], ihalf, True, True, inc=sT)
                    wait("scalar", sT, t_ib)
                    wait("scalar", sV, invb_read[h % 2])
                    s_cp = op("scalar", lambda e, h=h: e.copy(INVB[h % 2], P[2][:, 0:TT]), inc=sS)
                    p2_read_s = s_cp
                    wait("tensor", sV, p3_read_v)
                    t_qkv = mm(P[3][:, 0:TT], KVBF[:, g * HD:(g + 1) * HD], SQB[h % NSQ],
                               True, True, inc=sT)
                    wait("vector", sT, t_qkv)
                    wait("vector", sS, s_cp)
                    if t > 0 and h == 0:
                        wait("vector", sT, attn_done_t)
                        wait("vector", sT, prev_stats_t)
                        wait("vector", sS, prev_stats_s)
                    v_at = op("vector", lambda e, h=h: e.tensor_tensor(
                        ATTN[:, h * TT:(h + 1) * TT], P[3][:, 0:TT], INVB[h % 2], ALU.mult),
                        inc=sV)
                    p3_read_v = v_at
                    invb_read[h % 2] = v_at
                    sqb_qkv_t[h % NSQ] = t_qkv
                    attn_v[h] = v_at
                    k0 = h - 2
                    if k0 >= 0:
                        if k0 == 0:
                            wait("tensor", sV, pw_read[0])
                        wait("tensor", sV, attn_v[k0])
                        mm(P[4][:, 0:TT], WO[:, k0 * D:k0 * D + 128],
                           ATTN[:, k0 * TT:(k0 + 1) * TT], k0 == 0, False)
                    return v_at

                OFF = 6 if t == 0 else 5
                wo0_done = [0]
                attn_v = {}
                start_h = 0 if t == 0 else 4
                si = {}
                for h in range(start_h, H + OFF):
                    if h < H:
                        vsq[h] = do_q(h)
                    if 0 <= h - (OFF - 1) < H:
                        si[h - (OFF - 1)] = do_norm(h - (OFF - 1), vsq[h - (OFF - 1)])
                    if 0 <= h - OFF < H:
                        do_qkv(h - OFF, si[h - OFF])
                v_attn = sV.n

                for k0 in (H - 2, H - 1):
                    wait("tensor", sV, attn_v[k0])
                    wo0_done[0] = mm(P[4][:, 0:TT], WO[:, k0 * D:k0 * D + 128],
                                     ATTN[:, k0 * TT:(k0 + 1) * TT], False, k0 == H - 1,
                                     inc=(sT if k0 == H - 1 else None))

                wait("tensor", sV, v_attn)
                wait("vector", dXT, d_x)
                t_stats2 = s_sq2 = 0
                evac_vals = {}
                for m in range(NCH):
                    pw = P[4] if m % 2 == 0 else P[5]
                    if m == 0:
                        t_wo = wo0_done[0]
                    else:
                        if pw_read[m % 2]:
                            wait("tensor", sV, pw_read[m % 2])
                        t_wo = 0
                        for k in range(NCH):
                            t_wo = mm(pw[:, 0:TT], WO[:, k * D + m * 128:k * D + (m + 1) * 128],
                                      ATTN[:, k * TT:(k + 1) * TT], k == 0, k == NCH - 1,
                                      inc=(sT if k == NCH - 1 else None))
                    wait("vector", sT, t_wo)
                    pw_read[m % 2] = op("vector", lambda e, m=m, pw=pw: e.tensor_tensor(
                        XT[:, m * TT:(m + 1) * TT], pw[:, 0:TT],
                        XT[:, m * TT:(m + 1) * TT], ALU.add), inc=sV)
                    evac_vals[m] = pw_read[m % 2]
                    if m >= 2:
                        t_stats2, s_sq2 = ln_stats_chunk(
                            m - 2, P[6], P[7],
                            cast_wait=(lambda ev=evac_vals[m - 2]: wait("gpsimd", sV, ev)),
                            XBFd=XBFb, XSQd=XSQb)
                for mm_c in (NCH - 2, NCH - 1):
                    t_stats2, s_sq2 = ln_stats_chunk(
                        mm_c, P[6], P[7],
                        cast_wait=(lambda ev=evac_vals[mm_c]: wait("gpsimd", sV, ev)),
                        XBFd=XBFb, XSQd=XSQb)
                attn_done_t = sT.n
                v_x2 = sV.n
                prev_stats_t = t_stats2
                prev_stats_s = s_sq2

                v_fin2 = ln_finalize(P[6], P[7], t_stats2)
                wait("tensor", sV, v_fin2)
                t_bc2 = mm(P[6][:, 0:TT], ONESR[:], RSTD[:], True, True, inc=sT)
                wait("vector", sT, t_bc2)
                apply_v = {}
                for c in range(NCH):
                    apply_v[c] = op("vector", lambda e, c=c: e.tensor_tensor(
                        XLN[:, c * TT:(c + 1) * TT], XT[:, c * TT:(c + 1) * TT],
                        P[6][:, 0:TT], ALU.mult), inc=sV)
                v_l2s = sV.n

                if p2_read_s:
                    wait("tensor", sS, p2_read_s)
                wait("tensor", sV, p3_read_v)
                for c in range(NCH):
                    wait("tensor", sV, apply_v[c])
                    mm(P[2][0:8, 0:TT], FFN_RW[:, c * E8:(c + 1) * E8],
                       XLN[:, c * TT:(c + 1) * TT], c == 0, False)
                    mm(P[3][:, 0:TT], FFN_DW[:, c * EI:(c + 1) * EI],
                       XLN[:, c * TT:(c + 1) * TT], c == 0, False)
                t_lg = mm(P[2][0:8, 0:TT], WSRW[:], NMR[:], False, True, inc=sT)
                t_dn = mm(P[3][:, 0:TT], WSDW[:], NMR[:], False, True, inc=sT)
                if t + 1 < NT:
                    wait("sync", sT, t_dn)
                    d_xln = dma("sync", dXL,
                                XLN[:].rearrange("p (c t) -> p c t", t=TT),
                                xlnbf_h.rearrange("(c p) t -> p c t", p=128)[:, :, (t + 1) * TT:(t + 2) * TT])
                wait("scalar", sT, t_lg)
                s_lg = op("scalar", lambda e: e.copy(LG8[:], P[2][0:8, 0:TT]), inc=sS)
                wait("scalar", sT, t_dn)
                s_sil = op("scalar", lambda e: e.activation(EFF, P[3][:, 0:TT], AF.Exp, scale=-1.0),
                           inc=sS)
                wait("tensor", sS, s_lg)
                wait("tensor", sG, g_const)
                for c4 in range(4):
                    op("tensor", lambda e, c4=c4: e.transpose(
                        P[7][:, c4 * 8:(c4 + 1) * 8], LG8[:, c4 * 128:(c4 + 1) * 128],
                        ID32[0:8, 0:8]), inc=(sT if c4 == 3 else None))
                t_lgt = sT.n
                wait("vector", sT, t_lgt)

                def vstage(fns):
                    for i, fn in enumerate(fns):
                        op("vector", fn, inc=(sV if i == len(fns) - 1 else None))
                    wait("vector", sV)

                vstage([(lambda e, c4=c4: e.reduce_max(GM1[c4], P[7][:, c4 * 8:(c4 + 1) * 8],
                                                        axis=AX.X)) for c4 in range(4)])
                for c4 in range(4):
                    op("vector", lambda e, c4=c4: e.tensor_single_scalar(
                        WTS[c4], P[7][:, c4 * 8:(c4 + 1) * 8], GM1[c4], ALU.subtract),
                       inc=(sV if c4 == 3 else None))
                v_sub = sV.n
                wait("scalar", sV, v_sub)
                s_exp = 0
                for c4 in range(4):
                    s_exp = op("scalar", lambda e, c4=c4: e.activation(GT[c4], WTS[c4], AF.Exp),
                               inc=(sS if c4 == 3 else None))
                wait("vector", sS, s_exp)
                vstage([(lambda e, c4=c4: e.reduce_sum(GSM[c4], GT[c4], axis=AX.X))
                        for c4 in range(4)])
                vstage([(lambda e, c4=c4: e.reciprocal(GM1[c4], GSM[c4]))
                        for c4 in range(4)])
                vstage([(lambda e, c4=c4: e.tensor_single_scalar(WTS[c4], GT[c4], GM1[c4], ALU.mult))
                        for c4 in range(4)])
                vstage([(lambda e, c4=c4: e.reduce_max(GM1[c4], WTS[c4], axis=AX.X))
                        for c4 in range(4)])
                vstage([(lambda e, c4=c4: e.scalar_tensor_tensor(
                    LGT[c4], WTS[c4], GM1[c4], WTS[c4], ALU.is_lt, ALU.mult))
                        for c4 in range(4)])
                vstage([(lambda e, c4=c4: e.reduce_max(GM2[c4], LGT[c4], axis=AX.X))
                        for c4 in range(4)])
                vstage([(lambda e, c4=c4: e.scalar_tensor_tensor(
                    GT[c4], WTS[c4], GM2[c4], WTS[c4], ALU.is_ge, ALU.mult))
                        for c4 in range(4)])
                v_gate = sV.n
                wait("tensor", sV, v_gate)
                for c4 in range(4):
                    op("tensor", lambda e, c4=c4: e.transpose(
                        P[6][0:8, c4 * 128:(c4 + 1) * 128], GT[c4], ID32[:]),
                       inc=(sT if c4 == 3 else None))
                t_gf = sT.n
                wait("vector", sT, t_gf)
                v_gf8 = op("vector", lambda e: e.tensor_copy(GF8, P[6][0:8, 0:TT]), inc=sV)
                wait("tensor", sV, v_gf8)
                t_gb = mm(P[7][:, 0:TT], REP[:], GF8, True, True, inc=sT)
                wait("vector", sS, s_sil)
                op("vector", lambda e: e.tensor_scalar_add(EFF, EFF, 1.0), inc=sV)
                wait("vector", sV)
                op("vector", lambda e: e.reciprocal(RFF, EFF), inc=sV)
                wait("vector", sV)
                op("vector", lambda e: e.tensor_tensor(ABF, P[3][:, 0:TT], RFF, ALU.mult), inc=sV)
                wait("vector", sV)
                wait("vector", sT, t_gb)
                v_ab2 = op("vector", lambda e: e.tensor_tensor(AB2, ABF, P[7][:, 0:TT], ALU.mult),
                           inc=sV)
                p3_read_v = v_ab2
                p2_read_s = s_lg
                if t + 1 < NT:
                    p6row_read_v = v_gf8
                    for hh in range(4):
                        vsq[hh] = do_q(hh)
                wait("tensor", sV, v_ab2)
                for m in range(NCH):
                    pw = P[4] if m % 2 == 0 else P[5]
                    wait("tensor", sV, pw_read[m % 2])
                    t_up = mm(pw[:, 0:TT], FFN_UW[:, m * 128:(m + 1) * 128], AB2, True, True,
                              inc=sT)
                    wait("vector", sT, t_up)
                    pw_read[m % 2] = op("vector", lambda e, m=m, pw=pw: e.tensor_tensor(
                        XT[:, m * TT:(m + 1) * TT], pw[:, 0:TT],
                        XT[:, m * TT:(m + 1) * TT], ALU.add), inc=sV)
                v_out = sV.n

                wait("sync", sV, v_out)
                out_dma = dma("sync", dOUT,
                              out_h.rearrange("(c p) t -> p c t", p=128)[:, :, t * TT:(t + 1) * TT],
                              XT[:].rearrange("p (c t) -> p c t", t=TT))
                if t + 1 < NT:
                    wait("sync", dOUT, out_dma)
                    d_x = dma("sync", dXT,
                              XT[:].rearrange("p (c t) -> p c t", t=TT),
                              xin.rearrange("(c p) t -> p c t", p=128)[:, :, (t + 1) * TT:(t + 2) * TT])

            wait("sync", dOUT, out_dma)

            with nc.Block() as blk:
                @blk.sync
                def _(e):
                    for fn in plan["sync"]:
                        fn(e)

                @blk.tensor
                def _(e):
                    for fn in plan["tensor"]:
                        fn(e)

                @blk.vector
                def _(e):
                    for fn in plan["vector"]:
                        fn(e)

                @blk.scalar
                def _(e):
                    for fn in plan["scalar"]:
                        fn(e)

                @blk.gpsimd
                def _(e):
                    for fn in plan["gpsimd"]:
                        fn(e)
    return nc


_NC_CACHE = []


def kernel(L0_attn_hidden_states, L0_attn_Wq, L0_attn_bq, L0_attn_Wk, L0_attn_bk,
           L0_attn_Wv, L0_attn_bv, L0_attn_Wo, L0_ffn_router_w, L0_ffn_down_w,
           L0_ffn_up_w):
    x = np.asarray(L0_attn_hidden_states, dtype=np.float32).reshape(16384, 2048)
    wq = np.asarray(L0_attn_Wq, dtype=np.float32)
    wk = np.asarray(L0_attn_Wk, dtype=np.float32)
    wv = np.asarray(L0_attn_Wv, dtype=np.float32)
    wo = np.asarray(L0_attn_Wo, dtype=np.float32)
    rw = np.asarray(L0_ffn_router_w, dtype=np.float32)
    dw = np.asarray(L0_ffn_down_w, dtype=np.float32)
    uw = np.asarray(L0_ffn_up_w, dtype=np.float32)
    bq = np.asarray(L0_attn_bq, dtype=np.float32)
    bk = np.asarray(L0_attn_bk, dtype=np.float32)
    bv = np.asarray(L0_attn_bv, dtype=np.float32)

    wqt = np.ascontiguousarray(wq.T)
    wkvt = np.ascontiguousarray(np.concatenate([wk.T, wv.T], axis=1))
    wot = np.ascontiguousarray(wo.T)
    rwt = np.ascontiguousarray(rw.T)
    dwt = np.ascontiguousarray(dw.reshape(128, 2048).T)
    uwt = np.ascontiguousarray(uw.transpose(0, 2, 1).reshape(128, 2048))
    bqrow = np.ascontiguousarray(bq.reshape(1, 2048))
    bkvrow = np.ascontiguousarray(np.concatenate([bk, bv]).reshape(1, 1024))

    if not _NC_CACHE:
        _NC_CACHE.append(build_nc())
    nc = _NC_CACHE[0]
    in_maps = []
    for c in range(8):
        xt = np.ascontiguousarray(x[c * 2048:(c + 1) * 2048, :].T)
        in_maps.append({
            "XT": xt, "WQT": wqt, "WKVT": wkvt, "WOT": wot, "RWT": rwt,
            "DWT": dwt, "UWT": uwt, "BQROW": bqrow, "BKVROW": bkvrow,
        })
    res = run_bass_kernel_spmd(nc, in_maps, core_ids=list(range(8)))
    outs = [np.ascontiguousarray(res.results[c]["OUT"].T) for c in range(8)]
    return np.concatenate(outs, axis=0).reshape(2, 8192, 2048)


if __name__ == "__main__":
    import reference
    ins = reference.setup_inputs()
    out = kernel(**{k: np.asarray(v) for k, v in ins.items()})
    ref = np.asarray(reference.reference(**ins))
    err = float(np.linalg.norm((out - ref).ravel()) / np.linalg.norm(np.asarray(ref).ravel()))
    print("rel_l2:", err)


# version 18
# speedup vs baseline: 1.0361x; 1.0361x over previous
import contextlib
import numpy as np
import concourse.bass as bass
import concourse.mybir as mybir
from concourse.bass_utils import run_bass_kernel_spmd

F32 = mybir.dt.float32
BF16 = mybir.dt.bfloat16
AF = mybir.ActivationFunctionType
ALU = mybir.AluOpType
AX = mybir.AxisListType

D = 2048
NCH = 16
T = 2048
TT = 512
NT = T // TT
NU = TT // 128
H = 16
G = 4
HD = 128
E8 = 8
EI = 128
LN_EPS = 1e-5
AT_EPS = 1e-6


def build_nc():
    nc = bass.Bass("TRN2", target_bir_lowering=False)

    xin = nc.dram_tensor("XT", [D, T], F32, kind="ExternalInput").ap()
    wqt = nc.dram_tensor("WQT", [D, D], F32, kind="ExternalInput").ap()
    wkvt = nc.dram_tensor("WKVT", [D, 1024], F32, kind="ExternalInput").ap()
    wot = nc.dram_tensor("WOT", [D, D], F32, kind="ExternalInput").ap()
    rwt = nc.dram_tensor("RWT", [D, E8], F32, kind="ExternalInput").ap()
    dwt = nc.dram_tensor("DWT", [D, EI], F32, kind="ExternalInput").ap()
    uwt = nc.dram_tensor("UWT", [EI, D], F32, kind="ExternalInput").ap()
    bqrow = nc.dram_tensor("BQROW", [1, D], F32, kind="ExternalInput").ap()
    bkvrow = nc.dram_tensor("BKVROW", [1, 1024], F32, kind="ExternalInput").ap()
    out_h = nc.dram_tensor("OUT", [D, T], F32, kind="ExternalOutput").ap()
    wqbf_h = nc.dram_tensor("WQBF", [8, 128, 4096], BF16, kind="Internal").ap()
    xlnbf_h = nc.dram_tensor("XLNBF", [D, T], BF16, kind="Internal").ap()
    ccin = nc.dram_tensor("CCIN", [128, G * 129], F32, kind="Internal")
    ccout = nc.dram_tensor("CCOUT", [128, G * 129], F32, kind="Internal")
    ccin2 = nc.dram_tensor("CCIN2", [128, G * 129], F32, kind="Internal")
    ccout2 = nc.dram_tensor("CCOUT2", [128, G * 129], F32, kind="Internal")

    es = contextlib.ExitStack()
    sb = lambda name, shape, dt: es.enter_context(nc.sbuf_tensor(name, shape, dt))
    psf = lambda name: es.enter_context(nc.psum_tensor(name, [128, 512], F32))

    with es:
        XT = sb("XTs", [128, NCH * TT], F32)
        XLN = sb("XLN", [128, NCH * TT], BF16)
        ATTN = sb("ATTN", [128, H * TT], BF16)
        WO = sb("WO", [128, NCH * D], BF16)
        FFNW = sb("FFNW", [128, NCH * (E8 + EI) + D], BF16)
        KVBF = sb("KVBF", [128, G * HD], BF16)
        KSUM = sb("KSUM", [128, G], BF16)
        KVROWS = sb("KVROWS", [128, G * 129], F32)
        KVR1 = sb("KVR1", [128, G * 129], F32)
        RSTD = sb("RSTD", [1, TT], BF16)
        NMR = sb("NMR", [1, TT], BF16)
        INVNR = sb("INVNR", [1, 2 * TT], BF16)
        R32 = sb("R32", [1, 4 * TT], F32)
        ONESC = sb("ONESC", [128, 1], BF16)
        ONESR = sb("ONESR", [1, 128], BF16)
        ONES5 = sb("ONES5", [1, TT], BF16)
        REP = sb("REP", [8, 128], BF16)
        ID32 = sb("ID32", [128, 128], F32)
        BQR = sb("BQR", [1, D], BF16)
        BKVR = sb("BKVR", [1, 1024], BF16)
        WSRW = sb("WSRW", [1, E8], BF16)
        WSDW = sb("WSDW", [1, EI], BF16)
        NMRB = sb("NMRB", [128, TT], F32)
        LG8 = sb("LG8", [8, TT], F32)
        GS = sb("GS", [128, 128], F32)
        STG0 = sb("STG0", [128, TT], F32)
        STG1 = sb("STG1", [128, TT], F32)
        SCRB = sb("SCRB", [128, 18432], BF16)

        WKV = SCRB[:, 0:16384]
        SKB = [SCRB[:, 16384:16896], SCRB[:, 16896:17408]]
        WQH = [SCRB[:, s * 4096:(s + 1) * 4096] for s in range(2)]
        NSQ = 7
        SQB = [SCRB[:, 8192 + i * 512:8704 + i * 512] for i in range(NSQ)]
        INVB = [SCRB[:, 11776 + i * 512:12288 + i * 512] for i in range(2)]
        GF8 = SCRB[0:8, 12800:13312]
        ABF = SCRB[:, 13312:13824]
        AB2 = SCRB[:, 13824:14336]
        RSB = SCRB[:, 17408:17920]
        EXPB = SCRB[:, 17920:18432]
        EFF = STG0[:, 0:TT]
        RFF = STG1[:, 0:TT]

        XBFa = [ATTN[:, 0:512], ATTN[:, 512:1024]]
        XSQa = [ATTN[:, 1024:1536], ATTN[:, 1536:2048]]
        XBFb = [SCRB[:, 14336:14848], SCRB[:, 14848:15360]]
        XSQb = [SCRB[:, 15360:15872], SCRB[:, 15872:16384]]
        EXPa = [ATTN[:, 2048:2560], ATTN[:, 2560:3072]]
        RSBa = [ATTN[:, 3072:3584], ATTN[:, 3584:4096]]
        VOB = [ATTN[:, 4096:4616], ATTN[:, 4616:5136]]

        ROW0 = R32[:, 0:TT]
        ROW1 = R32[:, TT:2 * TT]
        ROW2 = R32[:, 2 * TT:3 * TT]
        ROW3 = R32[:, 3 * TT:4 * TT]

        FFN_RW = FFNW[:, 0:NCH * E8]
        FFN_DW = FFNW[:, NCH * E8:NCH * (E8 + EI)]
        FFN_UW = FFNW[:, NCH * (E8 + EI):NCH * (E8 + EI) + D]

        LGT = [GS[:, c * 8:(c + 1) * 8] for c in range(4)]
        WTS = [GS[:, 32 + c * 8:32 + (c + 1) * 8] for c in range(4)]
        GT = [GS[:, 64 + c * 8:64 + (c + 1) * 8] for c in range(4)]
        GM1 = [GS[:, 96 + c:97 + c] for c in range(4)]
        GM2 = [GS[:, 100 + c:101 + c] for c in range(4)]
        GSM = [GS[:, 104 + c:105 + c] for c in range(4)]

        P = [psf(f"P{i}") for i in range(8)]
        AKV = P[0:4]

        ses = contextlib.ExitStack()

        class Sem:
            def __init__(self, name):
                self.h = ses.enter_context(nc.semaphore(name))
                self.n = 0

        dXT = Sem("dXT")
        dXL = Sem("dXL")
        dWB = Sem("dWB")
        dOUT = Sem("dOUT")
        dCC = Sem("dCC")
        dCC2 = Sem("dCC2")
        dQs = [Sem("dQ0"), Sem("dQ1")]
        dGW = Sem("dGW")
        dGF = Sem("dGF")
        dGO = Sem("dGO")
        dGN = Sem("dGN")
        dGQ = [Sem("dGQ0"), Sem("dGQ1")]
        dGB = [Sem("dGB0"), Sem("dGB1")]
        sT = Sem("sT")
        sV = Sem("sV")
        sS = Sem("sS")
        sG = Sem("sG")
        sC = Sem("sC")

        with ses:
            plan = {k: [] for k in ("sync", "tensor", "vector", "scalar", "gpsimd")}

            def op(eng, fn, inc=None, inck=1):
                if inc is None:
                    plan[eng].append(lambda e, fn=fn: fn(e))
                    return None
                tgt = inc.n + inck
                inc.n = tgt
                plan[eng].append(lambda e, fn=fn, s=inc, k=inck: fn(e).then_inc(s.h, k))
                return tgt

            def wait(eng, sem, val=None):
                v = sem.n if val is None else val
                assert v <= sem.n, f"wait on future value {v} > {sem.n}"
                if v > 0:
                    plan[eng].append(lambda e, s=sem, v=v: e.wait_ge(s.h, v))
                return v

            def dma(q_eng, sem, outap, inap):
                return op(q_eng, lambda e, o=outap, i=inap: e.dma_start(out=o, in_=i),
                          inc=sem, inck=16)

            def mm(pq, lhsT, rhs, start, stop, inc=None):
                return op("tensor",
                          lambda e, o=pq, l=lhsT, r=rhs, a=start, b=stop:
                          e.matmul(o, l, r, start=a, stop=b), inc=inc)

            op("vector", lambda e: e.memset(ONESC[:], 1.0))
            op("vector", lambda e: e.memset(ONESR[:], 1.0))
            op("vector", lambda e: e.memset(ONES5[:], 1.0))
            op("vector", lambda e: e.memset(REP[:], 1.0))
            op("vector", lambda e: e.memset(ID32[:], 1.0))
            for i in range(2):
                op("vector", lambda e, i=i: e.memset(VOB[i], 0.0), inc=sV)
            wait("vector", sV)
            for i in range(2):
                for g in range(G):
                    op("vector", lambda e, i=i, g=g: e.memset(
                        VOB[i][:, g * 130 + 128:g * 130 + 129], 1.0),
                       inc=(sV if (i, g) == (1, G - 1) else None))
            v_const = sV.n
            wait("gpsimd", sV, v_const)
            op("gpsimd", lambda e: e.affine_select(
                REP[:].rearrange("p (j i) -> p j i", i=16),
                REP[:].rearrange("p (j i) -> p j i", i=16),
                pattern=[[-1, 8], [0, 16]], compare_op=ALU.is_equal, fill=0.0,
                base=0, channel_multiplier=1))
            op("gpsimd", lambda e: e.affine_select(
                ID32[:], ID32[:], pattern=[[-1, 128]], compare_op=ALU.is_equal,
                fill=0.0, base=0, channel_multiplier=1), inc=sG)
            g_const = sG.n

            wait("gpsimd", sG, g_const)
            for k in range(4):
                op("gpsimd", lambda e, k=k: e.dma_start(
                    out=BQR[:, k * TT:(k + 1) * TT], in_=bqrow[:, k * TT:(k + 1) * TT]),
                   inc=dGF, inck=16)
            for k in range(2):
                op("gpsimd", lambda e, k=k: e.dma_start(
                    out=BKVR[:, k * TT:(k + 1) * TT], in_=bkvrow[:, k * TT:(k + 1) * TT]),
                   inc=dGF, inck=16)
            for c in range(NCH):
                op("gpsimd", lambda e, c=c: e.dma_start(
                    out=WKV[:, c * 1024:(c + 1) * 1024], in_=wkvt[c * 128:(c + 1) * 128, :]),
                   inc=dGW, inck=16)
            d_wkv = dGW.n
            for c in range(NCH):
                op("gpsimd", lambda e, c=c: e.dma_start(
                    out=FFN_RW[:, c * E8:(c + 1) * E8], in_=rwt[c * 128:(c + 1) * 128, :]),
                   inc=dGF, inck=16)
                op("gpsimd", lambda e, c=c: e.dma_start(
                    out=FFN_DW[:, c * EI:(c + 1) * EI], in_=dwt[c * 128:(c + 1) * 128, :]),
                   inc=dGF, inck=16)
            op("gpsimd", lambda e: e.dma_start(out=FFN_UW, in_=uwt[0:128, :]), inc=dGF, inck=16)
            d_ffnw = dGF.n

            wait("tensor", dGF, d_ffnw)
            wait("tensor", sV, v_const)
            for c in range(NCH):
                mm(P[6][0:1, 0:E8], ONESC[:], FFN_RW[:, c * E8:(c + 1) * E8],
                   c == 0, c == NCH - 1)
            t_ws = 0
            for c in range(NCH):
                t_ws = mm(P[7][0:1, 0:EI], ONESC[:], FFN_DW[:, c * EI:(c + 1) * EI],
                          c == 0, c == NCH - 1, inc=(sT if c == NCH - 1 else None))
            wait("vector", sT, t_ws)
            op("vector", lambda e: e.tensor_copy(WSRW[:], P[6][0:1, 0:E8]))
            v_ws = op("vector", lambda e: e.tensor_copy(WSDW[:], P[7][0:1, 0:EI]), inc=sV)

            wo_c = [0]

            def wo_flow(sem=None):
                c = wo_c[0]
                wo_c[0] += 1
                op("gpsimd", lambda e, c=c: e.dma_start(
                    out=WO[:, c * D:(c + 1) * D], in_=wot[c * 128:(c + 1) * 128, :]),
                   inc=(sem or dGO), inck=16)

            stats_guard = {"sq": {0: 0, 1: 0}, "mm1": {0: 0, 1: 0}, "mm2": {0: 0, 1: 0},
                           "cast": {0: 0, 1: 0}}

            def ln_stats_chunk(c, pb1, pb2, pre_tensor_waits=None, cast_wait=None,
                               XBFd=XBFa, XSQd=XSQa):
                i = c % 2
                wait("gpsimd", sS, stats_guard["sq"][i])
                wait("gpsimd", sT, stats_guard["mm1"][i])
                if cast_wait is not None:
                    cast_wait()
                v_cast = op("gpsimd", lambda e, c=c, i=i: e.tensor_copy(
                    XBFd[i], XT[:, c * TT:(c + 1) * TT]), inc=sG)
                stats_guard["cast"][i] = v_cast
                wait("scalar", sG, v_cast)
                wait("scalar", sT, stats_guard["mm2"][i])
                last_sq = op("scalar", lambda e, i=i: e.activation(XSQd[i], XBFd[i], AF.Square),
                             inc=sS)
                stats_guard["sq"][i] = last_sq
                wait("tensor", sG, v_cast)
                wait("tensor", sS, last_sq)
                if pre_tensor_waits is not None and c == 0:
                    pre_tensor_waits()
                stats_guard["mm1"][i] = mm(pb1[0:1, 0:TT], ONESC[:], XBFd[i],
                                           c == 0, c == NCH - 1, inc=sT)
                last_mm2 = mm(pb2[0:1, 0:TT], ONESC[:], XSQd[i], c == 0, c == NCH - 1, inc=sT)
                stats_guard["mm2"][i] = last_mm2
                return last_mm2, last_sq

            def ln_stats(pb1, pb2, pre_tensor_waits=None, cast_wait=None):
                last_mm2 = last_sq = 0
                for c in range(NCH):
                    last_mm2, last_sq = ln_stats_chunk(c, pb1, pb2, pre_tensor_waits, cast_wait)
                return last_mm2, last_sq

            def ln_finalize(pb1, pb2, t_stats):
                wait("vector", sT, t_stats)
                op("vector", lambda e: e.tensor_scalar_mul(ROW0, pb1[0:1, 0:TT], 1.0 / D))
                op("vector", lambda e: e.tensor_scalar_mul(ROW1, pb2[0:1, 0:TT], 1.0 / D), inc=sV)
                wait("vector", sV)
                op("vector", lambda e: e.tensor_tensor(ROW2, ROW0, ROW0, ALU.mult), inc=sV)
                wait("vector", sV)
                op("vector", lambda e: e.tensor_tensor(ROW1, ROW1, ROW2, ALU.subtract), inc=sV)
                wait("vector", sV)
                v_var = op("vector", lambda e: e.tensor_scalar_add(ROW1, ROW1, LN_EPS), inc=sV)
                wait("scalar", sV, v_var)
                op("scalar", lambda e: e.activation(ROW2, ROW1, AF.Ln), inc=sS)
                wait("scalar", sS)
                s_rstd = op("scalar", lambda e: e.activation(RSTD[:], ROW2, AF.Exp, scale=-0.5),
                            inc=sS)
                wait("vector", sS, s_rstd)
                op("vector", lambda e: e.tensor_tensor(NMR[:], ROW0, RSTD[:], ALU.mult), inc=sV)
                wait("vector", sV)
                v_fin = op("vector", lambda e: e.tensor_scalar_mul(NMR[:], NMR[:], -1.0), inc=sV)
                return v_fin

            evac_s = {0: 0, 1: 0}
            evac_v = {0: 0, 1: 0}
            kvstate_done = {0: 0, 1: 0}
            kvstate_t = 0
            g_nmrb_read = 0
            wb_prev = 0

            d_x = dma("sync", dXT,
                      XT[:].rearrange("p (c t) -> p c t", t=TT),
                      xin.rearrange("(c p) t -> p c t", p=128)[:, :, 0:TT])
            wait("gpsimd", dXT, d_x)
            t_stats, _ = ln_stats(P[4], P[5])
            v_fin = ln_finalize(P[4], P[5], t_stats)

            cc_split = {}
            WOQ = [WO[:, 24576:28672], WO[:, 28672:32768]]
            wqh_qmm0 = [0, 0]
            wqh_dma0 = [0, 0]
            woq_tq = [0, 0]
            flow_dgo = {}

            def wqh_load(grp, dest=None):
                slot = grp % 2
                if wqh_qmm0[slot]:
                    wait("gpsimd", sT, wqh_qmm0[slot])
                d = dest if dest is not None else WQH[slot]
                wqh_dma0[slot] = op("gpsimd", lambda e, d=d, grp=grp: e.dma_start(
                    out=d.rearrange("p (c o) -> p c o", o=256),
                    in_=wqt.rearrange("(c p) o -> p c o", p=128)[:, :, grp * 256:(grp + 1) * 256]),
                    inc=dQs[slot], inck=16)

            for t in range(NT):
                if t == NT - 1:
                    wait("scalar", sT, kvstate_t)
                    for g in range(G):
                        op("scalar", lambda e, g=g: e.copy(
                            KVROWS[:, g * 129:(g + 1) * 129], AKV[g][:, 0:129]),
                           inc=(sS if g == G - 1 else None))
                    cc_split["evac1"] = sS.n
                    wait("scalar", sS, cc_split["evac1"])
                    cc_split["ccin1"] = dma("scalar", dCC, ccin.ap(), KVROWS[:])
                wait("tensor", sV, v_fin)
                mm(P[4][:, 0:TT], ONESR[:], RSTD[:], True, True)
                t_bc = mm(P[5][:, 0:TT], ONESR[:], NMR[:], True, True, inc=sT)
                wait("scalar", sT, t_bc)
                if t > 0:
                    wait("scalar", sG, g_nmrb_read)
                s_nmrb = op("scalar", lambda e: e.copy(NMRB[:], P[5][:, 0:TT]), inc=sS)
                wait("vector", sT, t_bc)
                if t > 0:
                    wait("vector", dWB, wb_prev)
                mult_v = {}
                for c in range(NCH):
                    mult_v[c] = op("vector", lambda e, c=c: e.tensor_tensor(
                        XLN[:, c * TT:(c + 1) * TT], XT[:, c * TT:(c + 1) * TT],
                        P[4][:, 0:TT], ALU.mult), inc=sV)
                v_scale = mult_v[NCH - 1]
                add_sem = {}
                wait("gpsimd", sS, s_nmrb)
                for c in range(8):
                    wait("gpsimd", sV, mult_v[c])
                    add_sem[c] = (sG, op("gpsimd", lambda e, c=c: e.tensor_tensor(
                        XLN[:, c * TT:(c + 1) * TT], XLN[:, c * TT:(c + 1) * TT],
                        NMRB[:], ALU.add), inc=sG))
                g_ln1 = sG.n
                g_nmrb_read = g_ln1
                if t == NT - 1:
                    wqh_load(0, dest=WOQ[0])
                    wqh_load(1, dest=WOQ[1])
                wait("vector", sS, s_nmrb)
                for c in range(8, NCH):
                    wait("vector", sV, mult_v[c])
                    add_sem[c] = (sV, op("vector", lambda e, c=c: e.tensor_tensor(
                        XLN[:, c * TT:(c + 1) * TT], XLN[:, c * TT:(c + 1) * TT],
                        NMRB[:], ALU.add), inc=sV))
                v_ln1 = sV.n
                wait("sync", sG, g_ln1)
                wait("sync", sV, v_ln1)
                wb_prev = dma("sync", dWB,
                    xlnbf_h.rearrange("(c p) t -> p c t", p=128)[:, :, t * TT:(t + 1) * TT],
                    XLN[:].rearrange("p (c t) -> p c t", t=TT))
                if t + 1 < NT:
                    wait("sync", sV, v_scale)
                    d_x = dma("sync", dXT,
                              XT[:].rearrange("p (c t) -> p c t", t=TT),
                              xin.rearrange("(c p) t -> p c t", p=128)[:, :, (t + 1) * TT:(t + 2) * TT])

                if t == 0:
                    wait("tensor", dGW, d_wkv)
                    wait("tensor", dGF, d_ffnw)
                tkv = {}
                for u in range(NU):
                    kb = P[6] if u % 2 == 0 else P[4]
                    vb = P[7] if u % 2 == 0 else P[5]
                    i = u % 2
                    wait("tensor", sS, evac_s[i])
                    wait("tensor", sV, evac_v[i])
                    tok = u * 128
                    for c in range(NCH):
                        if u == 0:
                            wait("tensor", add_sem[c][0], add_sem[c][1])
                        lhs = XLN[:, c * TT + tok:c * TT + tok + 128]
                        mm(kb[:, 0:512], lhs, WKV[:, c * 1024:c * 1024 + 512], c == 0, False)
                        mm(vb[:, 0:512], lhs, WKV[:, c * 1024 + 512:(c + 1) * 1024], c == 0, False)
                    mm(kb[:, 0:512], ONESR[:], BKVR[:, 0:512], False, True)
                    tkv[u] = mm(vb[:, 0:512], ONESR[:], BKVR[:, 512:1024], False, True, inc=sT)
                    if u > 0:
                        pi = (u - 1) % 2
                        wait("tensor", sV, evac_v[pi])
                        first = (u == 1 and t in (0, NT - 1))
                        if t == NT - 1 and u == 1:
                            wait("tensor", sS, cc_split["evac1"])
                        for g in range(G):
                            kvstate_t = mm(AKV[g][:, 0:129], SKB[pi][:, g * 128:(g + 1) * 128],
                                           VOB[pi][:, g * 130:g * 130 + 129], first, False,
                                           inc=(sT if g == G - 1 else None))
                        kvstate_done[pi] = kvstate_t
                    wait("vector", sT, tkv[u])
                    wait("vector", sT, kvstate_done[i])
                    op("vector", lambda e, vb=vb, i=i: e.tensor_copy(
                        VOB[i].rearrange("p (g c) -> p g c", c=130)[:, :, 0:128],
                        vb[:, 0:512].rearrange("p (g c) -> p g c", c=128)))
                    wait("vector", sV, evac_v[i])
                    v_rsa = op("vector", lambda e, kb=kb, i=i: e.tensor_scalar_max(
                        RSBa[i], kb[:, 0:512], 0.0), inc=sV)
                    wait("scalar", sV, v_rsa)
                    s_ev = op("scalar", lambda e, kb=kb, i=i: e.activation(
                        EXPa[i], kb[:, 0:512], AF.Exp), inc=sS)
                    evac_s[i] = s_ev
                    wait("vector", sS, s_ev)
                    ev = op("vector", lambda e, i=i: e.scalar_tensor_tensor(
                        SKB[i], EXPa[i], 1.0, RSBa[i], ALU.min, ALU.add), inc=sV)
                    evac_v[i] = ev
                pi = (NU - 1) % 2
                wait("tensor", sV, evac_v[pi])
                last = (t >= NT - 2)
                for g in range(G):
                    kvstate_t = mm(AKV[g][:, 0:129], SKB[pi][:, g * 128:(g + 1) * 128],
                                   VOB[pi][:, g * 130:g * 130 + 129], False, last,
                                   inc=(sT if g == G - 1 else None))
                kvstate_done[pi] = kvstate_t

                if t + 1 < NT:
                    ev1, es1 = evac_v[1], evac_s[1]
                    wait("gpsimd", dXT, d_x)
                    t_stats, _ = ln_stats(
                        P[4], P[5],
                        pre_tensor_waits=(lambda ev1=ev1, es1=es1: (
                            wait("tensor", sV, ev1), wait("tensor", sS, es1))))
                    wait("vector", sT, t_bc)
                    v_fin = ln_finalize(P[4], P[5], t_stats)

                if t < NT - 1:
                    for _ in range(4):
                        wo_flow()
                if t == NT - 1:
                    wait("gpsimd", dCC, cc_split["ccin1"])
                    cc1 = op("gpsimd", lambda e: e.collective_compute(
                        "AllReduce", ALU.add, replica_groups=[[0, 1, 2, 3], [4, 5, 6, 7]],
                        ins=[ccin.ap()], outs=[ccout.ap()]), inc=sC)
                    wait("gpsimd", sC, cc1)
                    cc_split["ccout1"] = op("gpsimd", lambda e: e.dma_start(
                        out=KVR1[:], in_=ccout.ap()), inc=dCC2, inck=16)

            wait("vector", sT, kvstate_t)
            wait("vector", dCC, cc_split["ccin1"])
            for g in range(G):
                op("vector", lambda e, g=g: e.tensor_copy(
                    KVROWS[:, g * 129:(g + 1) * 129], AKV[g][:, 0:129]),
                   inc=(sV if g == G - 1 else None))
            v_kvrows = sV.n
            wait("scalar", sV, v_kvrows)
            d_ccin = dma("scalar", dCC, ccin2.ap(), KVROWS[:])

            wait("sync", sV, v_scale)
            d_x0 = dma("sync", dXT,
                       XT[:].rearrange("p (c t) -> p c t", t=TT),
                       xin.rearrange("(c p) t -> p c t", p=128)[:, :, 0:TT])
            wait("sync", sT, tkv[NU - 1])
            wait("sync", dWB, wb_prev)
            d_xln0 = dma("sync", dXL,
                         XLN[:].rearrange("p (c t) -> p c t", t=TT),
                         xlnbf_h.rearrange("(c p) t -> p c t", p=128)[:, :, 0:TT])

            fT, fV, fS, fG = sT.n, sV.n, sS.n, sG.n

            cc_state = {}

            def emit_cc():
                wait("gpsimd", dCC, d_ccin)
                cc = op("gpsimd", lambda e: e.collective_compute(
                    "AllReduce", ALU.add, replica_groups=[[0, 1, 2, 3], [4, 5, 6, 7]],
                    ins=[ccin2.ap()], outs=[ccout2.ap()]), inc=sC)
                wait("gpsimd", sC, cc)
                d_ccout = op("gpsimd", lambda e: e.dma_start(out=KVROWS[:], in_=ccout2.ap()),
                             inc=dCC2, inck=16)
                wait("gpsimd", dCC2, d_ccout)
                g_sum = op("gpsimd", lambda e: e.tensor_tensor(
                    KVROWS[:], KVROWS[:], KVR1[:], ALU.add), inc=sG)
                wait("gpsimd", sG, g_sum)
                for g in range(G):
                    op("gpsimd", lambda e, g=g: e.tensor_copy(
                        KVBF[:, g * HD:(g + 1) * HD], KVROWS[:, g * 129:g * 129 + 128]))
                cc_state["v_kv"] = op("gpsimd", lambda e: e.tensor_copy(
                    KSUM[:], KVROWS[:].rearrange("p (g c) -> p g c", c=129)[:, :, 128]), inc=sG)

            emit_cc()

            for eng in ("tensor", "vector", "scalar", "sync", "gpsimd"):
                wait(eng, sT, fT)
                wait(eng, sV, fV)
                wait(eng, sS, fS)
                if eng != "gpsimd":
                    wait(eng, sG, fG)
            wait("tensor", dGO)
            wait("sync", dWB)

            xt_free = (None, 0)
            xln_rd = (None, 0)
            out_dma = 0
            pw_read = {0: 0, 1: 0}
            pq_read = {0: (0, 0), 1: (0, 0)}
            prev_stats_t = 0
            prev_stats_s = 0
            p2_read_s = 0
            p3_read_v = 0
            p6row_read_v = 0
            invb_read = {0: 0, 1: 0}
            sqb_qkv_t = {i: 0 for i in range(NSQ)}
            expb_read_v = 0
            attn_done_t = 0

            wait("tensor", sV, v_kvrows)
            vsq = {}
            d_x, d_xln = d_x0, d_xln0
            for t in range(NT):

                def do_q(h, spliced=False):
                    nonlocal expb_read_v
                    slot = (h // 2) % 2
                    j = h % 2
                    if j == 0 and not (t == 0 and h == 0 and not spliced):
                        wqh_load((h // 2 + 1) % 8)
                    pq = P[h % 2]
                    rS, rV = pq_read[h % 2]
                    wait("tensor", sS, rS)
                    wait("tensor", sV, rV)
                    wait("tensor", dQs[slot], wqh_dma0[slot])
                    if h == 0:
                        wait("tensor", dXL, d_xln)
                    wsl = WOQ[slot] if (t == 0 and h < 4 and not spliced) else WQH[slot]
                    for c in range(NCH):
                        lhs = wsl[:, c * 256 + j * 128:c * 256 + (j + 1) * 128]
                        mm(pq[:, 0:TT], lhs, XLN[:, c * TT:(c + 1) * TT], c == 0, False)
                    t_q = mm(pq[:, 0:TT], BQR[:, h * 128:(h + 1) * 128], ONES5[:], False, True,
                             inc=sT)
                    wqh_qmm0[slot] = t_q
                    if t == 0 and h in (1, 3):
                        woq_tq[h // 2] = t_q
                    wait("vector", sT, t_q)
                    wait("vector", sT, sqb_qkv_t[h % NSQ])
                    wait("vector", sV, expb_read_v)
                    v_rsb = op("vector", lambda e, pq=pq: e.tensor_scalar_max(RSB, pq[:, 0:TT], 0.0),
                               inc=sV)
                    wait("scalar", sV, v_rsb)
                    s_q = op("scalar", lambda e, pq=pq: e.activation(EXPB, pq[:, 0:TT], AF.Exp),
                             inc=sS)
                    wait("vector", sS, s_q)
                    v_sq = op("vector", lambda e, h=h: e.scalar_tensor_tensor(
                        SQB[h % NSQ], EXPB, 1.0, RSB, ALU.min, ALU.add), inc=sV)
                    expb_read_v = v_sq
                    pq_read[h % 2] = (s_q, v_sq)
                    return v_sq

                def do_norm(h, v_sq):
                    nonlocal p6row_read_v
                    g = h // 4
                    wait("tensor", sV, v_sq)
                    if t == 0 and h == 0:
                        wait("tensor", sG, cc_state["v_kv"])
                    wait("tensor", sV, p6row_read_v)
                    t_n = mm(P[6][0:1, 0:TT], KSUM[:, g:g + 1], SQB[h % NSQ], True, True, inc=sT)
                    nrow = ROW0 if h % 2 == 0 else ROW1
                    lrow = ROW2 if h % 2 == 0 else ROW3
                    ihalf = INVNR[:, (h % 2) * TT:(h % 2 + 1) * TT]
                    wait("vector", sT, t_n)
                    v_n = op("vector", lambda e, nrow=nrow: e.tensor_scalar_add(
                        nrow, P[6][0:1, 0:TT], AT_EPS), inc=sV)
                    p6row_read_v = v_n
                    wait("scalar", sV, v_n)
                    op("scalar", lambda e, nrow=nrow, lrow=lrow: e.activation(lrow, nrow, AF.Ln),
                       inc=sS)
                    wait("scalar", sS)
                    s_i = op("scalar", lambda e, lrow=lrow, ih=ihalf: e.activation(
                        ih, lrow, AF.Exp, scale=-1.0), inc=sS)
                    return s_i

                def do_qkv(h, s_i):
                    nonlocal p2_read_s, p3_read_v
                    g = h // 4
                    ihalf = INVNR[:, (h % 2) * TT:(h % 2 + 1) * TT]
                    wait("tensor", sS, s_i)
                    wait("tensor", sS, p2_read_s)
                    t_ib = mm(P[2][:, 0:TT], ONESR[:], ihalf, True, True, inc=sT)
                    wait("scalar", sT, t_ib)
                    wait("scalar", sV, invb_read[h % 2])
                    s_cp = op("scalar", lambda e, h=h: e.copy(INVB[h % 2], P[2][:, 0:TT]), inc=sS)
                    p2_read_s = s_cp
                    wait("tensor", sV, p3_read_v)
                    t_qkv = mm(P[3][:, 0:TT], KVBF[:, g * HD:(g + 1) * HD], SQB[h % NSQ],
                               True, True, inc=sT)
                    wait("vector", sT, t_qkv)
                    wait("vector", sS, s_cp)
                    if t > 0 and h == 0:
                        wait("vector", sT, attn_done_t)
                        wait("vector", sT, prev_stats_t)
                        wait("vector", sS, prev_stats_s)
                    v_at = op("vector", lambda e, h=h: e.tensor_tensor(
                        ATTN[:, h * TT:(h + 1) * TT], P[3][:, 0:TT], INVB[h % 2], ALU.mult),
                        inc=sV)
                    p3_read_v = v_at
                    invb_read[h % 2] = v_at
                    sqb_qkv_t[h % NSQ] = t_qkv
                    attn_v[h] = v_at
                    k0 = h - 2
                    if k0 >= 0:
                        if k0 == 0:
                            wait("tensor", sV, pw_read[0])
                        if t == 0 and k0 >= 12:
                            wait("tensor", dGN, flow_dgo[k0])
                        wait("tensor", sV, attn_v[k0])
                        mm(P[4][:, 0:TT], WO[:, k0 * D:k0 * D + 128],
                           ATTN[:, k0 * TT:(k0 + 1) * TT], k0 == 0, False)
                    return v_at

                OFF = 6 if t == 0 else 5
                wo0_done = [0]
                attn_v = {}
                start_h = 0 if t == 0 else 4
                si = {}
                for h in range(start_h, H + OFF):
                    if h < H:
                        vsq[h] = do_q(h)
                    if t == 0 and h in (6, 8, 10, 12):
                        wait("gpsimd", sT, woq_tq[(h - 6) // 4])
                        if h > 6:
                            wait("gpsimd", dGN, dGN.n)
                        wo_flow(sem=dGN)
                        flow_dgo[12 + (h - 6) // 2] = dGN.n
                    if 0 <= h - (OFF - 1) < H:
                        si[h - (OFF - 1)] = do_norm(h - (OFF - 1), vsq[h - (OFF - 1)])
                    if 0 <= h - OFF < H:
                        do_qkv(h - OFF, si[h - OFF])
                v_attn = sV.n

                for k0 in (H - 2, H - 1):
                    if t == 0:
                        wait("tensor", dGN, flow_dgo[k0])
                    wait("tensor", sV, attn_v[k0])
                    wo0_done[0] = mm(P[4][:, 0:TT], WO[:, k0 * D:k0 * D + 128],
                                     ATTN[:, k0 * TT:(k0 + 1) * TT], False, k0 == H - 1,
                                     inc=(sT if k0 == H - 1 else None))

                wait("tensor", sV, v_attn)
                wait("vector", dXT, d_x)
                t_stats2 = s_sq2 = 0
                evac_vals = {}
                for m in range(NCH):
                    pw = P[4] if m % 2 == 0 else P[5]
                    if m == 0:
                        t_wo = wo0_done[0]
                    else:
                        if pw_read[m % 2]:
                            wait("tensor", sV, pw_read[m % 2])
                        t_wo = 0
                        for k in range(NCH):
                            t_wo = mm(pw[:, 0:TT], WO[:, k * D + m * 128:k * D + (m + 1) * 128],
                                      ATTN[:, k * TT:(k + 1) * TT], k == 0, k == NCH - 1,
                                      inc=(sT if k == NCH - 1 else None))
                    wait("vector", sT, t_wo)
                    pw_read[m % 2] = op("vector", lambda e, m=m, pw=pw: e.tensor_tensor(
                        XT[:, m * TT:(m + 1) * TT], pw[:, 0:TT],
                        XT[:, m * TT:(m + 1) * TT], ALU.add), inc=sV)
                    evac_vals[m] = pw_read[m % 2]
                    if m >= 2:
                        t_stats2, s_sq2 = ln_stats_chunk(
                            m - 2, P[6], P[7],
                            cast_wait=(lambda ev=evac_vals[m - 2]: wait("gpsimd", sV, ev)),
                            XBFd=XBFb, XSQd=XSQb)
                for mm_c in (NCH - 2, NCH - 1):
                    t_stats2, s_sq2 = ln_stats_chunk(
                        mm_c, P[6], P[7],
                        cast_wait=(lambda ev=evac_vals[mm_c]: wait("gpsimd", sV, ev)),
                        XBFd=XBFb, XSQd=XSQb)
                attn_done_t = sT.n
                v_x2 = sV.n
                prev_stats_t = t_stats2
                prev_stats_s = s_sq2

                v_fin2 = ln_finalize(P[6], P[7], t_stats2)
                wait("tensor", sV, v_fin2)
                t_bc2 = mm(P[6][:, 0:TT], ONESR[:], RSTD[:], True, True, inc=sT)
                wait("vector", sT, t_bc2)
                apply_v = {}
                for c in range(NCH):
                    apply_v[c] = op("vector", lambda e, c=c: e.tensor_tensor(
                        XLN[:, c * TT:(c + 1) * TT], XT[:, c * TT:(c + 1) * TT],
                        P[6][:, 0:TT], ALU.mult), inc=sV)
                v_l2s = sV.n

                if p2_read_s:
                    wait("tensor", sS, p2_read_s)
                wait("tensor", sV, p3_read_v)
                for c in range(NCH):
                    wait("tensor", sV, apply_v[c])
                    mm(P[2][0:8, 0:TT], FFN_RW[:, c * E8:(c + 1) * E8],
                       XLN[:, c * TT:(c + 1) * TT], c == 0, False)
                    mm(P[3][:, 0:TT], FFN_DW[:, c * EI:(c + 1) * EI],
                       XLN[:, c * TT:(c + 1) * TT], c == 0, False)
                t_lg = mm(P[2][0:8, 0:TT], WSRW[:], NMR[:], False, True, inc=sT)
                t_dn = mm(P[3][:, 0:TT], WSDW[:], NMR[:], False, True, inc=sT)
                if t + 1 < NT:
                    wait("sync", sT, t_dn)
                    d_xln = dma("sync", dXL,
                                XLN[:].rearrange("p (c t) -> p c t", t=TT),
                                xlnbf_h.rearrange("(c p) t -> p c t", p=128)[:, :, (t + 1) * TT:(t + 2) * TT])
                wait("scalar", sT, t_lg)
                s_lg = op("scalar", lambda e: e.copy(LG8[:], P[2][0:8, 0:TT]), inc=sS)
                wait("scalar", sT, t_dn)
                s_sil = op("scalar", lambda e: e.activation(EFF, P[3][:, 0:TT], AF.Exp, scale=-1.0),
                           inc=sS)
                wait("tensor", sS, s_lg)
                wait("tensor", sG, g_const)
                for c4 in range(4):
                    op("tensor", lambda e, c4=c4: e.transpose(
                        P[7][:, c4 * 8:(c4 + 1) * 8], LG8[:, c4 * 128:(c4 + 1) * 128],
                        ID32[0:8, 0:8]), inc=(sT if c4 == 3 else None))
                t_lgt = sT.n
                wait("vector", sT, t_lgt)

                def vstage(fns):
                    for i, fn in enumerate(fns):
                        op("vector", fn, inc=(sV if i == len(fns) - 1 else None))
                    wait("vector", sV)

                vstage([(lambda e, c4=c4: e.reduce_max(GM1[c4], P[7][:, c4 * 8:(c4 + 1) * 8],
                                                        axis=AX.X)) for c4 in range(4)])
                for c4 in range(4):
                    op("vector", lambda e, c4=c4: e.tensor_single_scalar(
                        WTS[c4], P[7][:, c4 * 8:(c4 + 1) * 8], GM1[c4], ALU.subtract),
                       inc=(sV if c4 == 3 else None))
                v_sub = sV.n
                wait("scalar", sV, v_sub)
                s_exp = 0
                for c4 in range(4):
                    s_exp = op("scalar", lambda e, c4=c4: e.activation(GT[c4], WTS[c4], AF.Exp),
                               inc=(sS if c4 == 3 else None))
                wait("vector", sS, s_exp)
                vstage([(lambda e, c4=c4: e.reduce_sum(GSM[c4], GT[c4], axis=AX.X))
                        for c4 in range(4)])
                vstage([(lambda e, c4=c4: e.reciprocal(GM1[c4], GSM[c4]))
                        for c4 in range(4)])
                vstage([(lambda e, c4=c4: e.tensor_single_scalar(WTS[c4], GT[c4], GM1[c4], ALU.mult))
                        for c4 in range(4)])
                vstage([(lambda e, c4=c4: e.reduce_max(GM1[c4], WTS[c4], axis=AX.X))
                        for c4 in range(4)])
                vstage([(lambda e, c4=c4: e.scalar_tensor_tensor(
                    LGT[c4], WTS[c4], GM1[c4], WTS[c4], ALU.is_lt, ALU.mult))
                        for c4 in range(4)])
                vstage([(lambda e, c4=c4: e.reduce_max(GM2[c4], LGT[c4], axis=AX.X))
                        for c4 in range(4)])
                vstage([(lambda e, c4=c4: e.scalar_tensor_tensor(
                    GT[c4], WTS[c4], GM2[c4], WTS[c4], ALU.is_ge, ALU.mult))
                        for c4 in range(4)])
                v_gate = sV.n
                wait("tensor", sV, v_gate)
                for c4 in range(4):
                    op("tensor", lambda e, c4=c4: e.transpose(
                        P[6][0:8, c4 * 128:(c4 + 1) * 128], GT[c4], ID32[:]),
                       inc=(sT if c4 == 3 else None))
                t_gf = sT.n
                wait("vector", sT, t_gf)
                v_gf8 = op("vector", lambda e: e.tensor_copy(GF8, P[6][0:8, 0:TT]), inc=sV)
                wait("tensor", sV, v_gf8)
                t_gb = mm(P[7][:, 0:TT], REP[:], GF8, True, True, inc=sT)
                wait("vector", sS, s_sil)
                op("vector", lambda e: e.tensor_scalar_add(EFF, EFF, 1.0), inc=sV)
                wait("vector", sV)
                op("vector", lambda e: e.reciprocal(RFF, EFF), inc=sV)
                wait("vector", sV)
                op("vector", lambda e: e.tensor_tensor(ABF, P[3][:, 0:TT], RFF, ALU.mult), inc=sV)
                wait("vector", sV)
                wait("vector", sT, t_gb)
                v_ab2 = op("vector", lambda e: e.tensor_tensor(AB2, ABF, P[7][:, 0:TT], ALU.mult),
                           inc=sV)
                p3_read_v = v_ab2
                p2_read_s = s_lg
                if t + 1 < NT:
                    p6row_read_v = v_gf8
                    for hh in range(4):
                        vsq[hh] = do_q(hh, spliced=True)
                wait("tensor", sV, v_ab2)
                for m in range(NCH):
                    pw = P[4] if m % 2 == 0 else P[5]
                    wait("tensor", sV, pw_read[m % 2])
                    t_up = mm(pw[:, 0:TT], FFN_UW[:, m * 128:(m + 1) * 128], AB2, True, True,
                              inc=sT)
                    wait("vector", sT, t_up)
                    pw_read[m % 2] = op("vector", lambda e, m=m, pw=pw: e.tensor_tensor(
                        XT[:, m * TT:(m + 1) * TT], pw[:, 0:TT],
                        XT[:, m * TT:(m + 1) * TT], ALU.add), inc=sV)
                v_out = sV.n

                wait("sync", sV, v_out)
                out_dma = dma("sync", dOUT,
                              out_h.rearrange("(c p) t -> p c t", p=128)[:, :, t * TT:(t + 1) * TT],
                              XT[:].rearrange("p (c t) -> p c t", t=TT))
                if t + 1 < NT:
                    wait("sync", dOUT, out_dma)
                    d_x = dma("sync", dXT,
                              XT[:].rearrange("p (c t) -> p c t", t=TT),
                              xin.rearrange("(c p) t -> p c t", p=128)[:, :, (t + 1) * TT:(t + 2) * TT])

            wait("sync", dOUT, out_dma)

            with nc.Block() as blk:
                @blk.sync
                def _(e):
                    for fn in plan["sync"]:
                        fn(e)

                @blk.tensor
                def _(e):
                    for fn in plan["tensor"]:
                        fn(e)

                @blk.vector
                def _(e):
                    for fn in plan["vector"]:
                        fn(e)

                @blk.scalar
                def _(e):
                    for fn in plan["scalar"]:
                        fn(e)

                @blk.gpsimd
                def _(e):
                    for fn in plan["gpsimd"]:
                        fn(e)
    return nc


_NC_CACHE = []


def kernel(L0_attn_hidden_states, L0_attn_Wq, L0_attn_bq, L0_attn_Wk, L0_attn_bk,
           L0_attn_Wv, L0_attn_bv, L0_attn_Wo, L0_ffn_router_w, L0_ffn_down_w,
           L0_ffn_up_w):
    x = np.asarray(L0_attn_hidden_states, dtype=np.float32).reshape(16384, 2048)
    wq = np.asarray(L0_attn_Wq, dtype=np.float32)
    wk = np.asarray(L0_attn_Wk, dtype=np.float32)
    wv = np.asarray(L0_attn_Wv, dtype=np.float32)
    wo = np.asarray(L0_attn_Wo, dtype=np.float32)
    rw = np.asarray(L0_ffn_router_w, dtype=np.float32)
    dw = np.asarray(L0_ffn_down_w, dtype=np.float32)
    uw = np.asarray(L0_ffn_up_w, dtype=np.float32)
    bq = np.asarray(L0_attn_bq, dtype=np.float32)
    bk = np.asarray(L0_attn_bk, dtype=np.float32)
    bv = np.asarray(L0_attn_bv, dtype=np.float32)

    wqt = np.ascontiguousarray(wq.T)
    wkvt = np.ascontiguousarray(np.concatenate([wk.T, wv.T], axis=1))
    wot = np.ascontiguousarray(wo.T)
    rwt = np.ascontiguousarray(rw.T)
    dwt = np.ascontiguousarray(dw.reshape(128, 2048).T)
    uwt = np.ascontiguousarray(uw.transpose(0, 2, 1).reshape(128, 2048))
    bqrow = np.ascontiguousarray(bq.reshape(1, 2048))
    bkvrow = np.ascontiguousarray(np.concatenate([bk, bv]).reshape(1, 1024))

    if not _NC_CACHE:
        _NC_CACHE.append(build_nc())
    nc = _NC_CACHE[0]
    in_maps = []
    for c in range(8):
        xt = np.ascontiguousarray(x[c * 2048:(c + 1) * 2048, :].T)
        in_maps.append({
            "XT": xt, "WQT": wqt, "WKVT": wkvt, "WOT": wot, "RWT": rwt,
            "DWT": dwt, "UWT": uwt, "BQROW": bqrow, "BKVROW": bkvrow,
        })
    res = run_bass_kernel_spmd(nc, in_maps, core_ids=list(range(8)))
    outs = [np.ascontiguousarray(res.results[c]["OUT"].T) for c in range(8)]
    return np.concatenate(outs, axis=0).reshape(2, 8192, 2048)


if __name__ == "__main__":
    import reference
    ins = reference.setup_inputs()
    out = kernel(**{k: np.asarray(v) for k, v in ins.items()})
    ref = np.asarray(reference.reference(**ins))
    err = float(np.linalg.norm((out - ref).ravel()) / np.linalg.norm(np.asarray(ref).ravel()))
    print("rel_l2:", err)
